# version 14
# baseline (speedup 1.0000x reference)
import numpy as np
import concourse.bass as bass
import concourse.mybir as mybir
from concourse.bass_utils import run_bass_kernel_spmd

F32 = mybir.dt.float32
BF16 = mybir.dt.bfloat16
AF = mybir.ActivationFunctionType
ALU = mybir.AluOpType
AX = mybir.AxisListType.X

T = 2048
D = 1024
NH = 16
DH = 64
E = 8
F = 4096
NFC = 8
NF1C = 32
NTT = 16
EPS = 1e-5

ENGINES = ("tensor", "scalar", "vector", "gpsimd", "sync")


class Plan:

    def __init__(self, nc):
        self.nc = nc
        self.streams = {e: [] for e in ENGINES}
        self.sem = {}
        self.cnt = {}
        self.waited = {e: {} for e in ENGINES}

    def mksem(self, *names):
        for n in names:
            self.sem[n] = self.nc.alloc_semaphore(n)
            self.cnt[n] = 0

    def wait(self, eng, name, val=None):
        v = self.cnt[name] if val is None else val
        if v <= 0 or self.waited[eng].get(name, -1) >= v:
            return
        self.waited[eng][name] = v
        s = self.sem[name]
        self.streams[eng].append(lambda En, s=s, v=v: En.wait_ge(s, v))

    SELF_SEM = {"vector": "dve", "scalar": "act"}

    def do(self, eng, fn, inc=None, inc_by=1):
        own = self.SELF_SEM.get(eng)
        if own is not None:
            self.wait(eng, own)
            if inc is None:
                inc, inc_by = own, 1
        if inc is None:
            self.streams[eng].append(fn)
            return None
        self.cnt[inc] += inc_by
        tok = self.cnt[inc]
        s = self.sem[inc]
        self.streams[eng].append(
            lambda En, fn=fn, s=s, n=inc_by: fn(En).then_inc(s, n)
        )
        return tok


def build_nc(phases="ABC", dbg=False):
    nc = bass.Bass("TRN2", target_bir_lowering=False)

    d_x = nc.dram_tensor("x", [T, D], F32, kind="ExternalInput").ap()
    d_wvf = nc.dram_tensor("wvf", [NFC, NFC, 128, 128], F32, kind="ExternalInput").ap()
    d_bvc = nc.dram_tensor("bvc", [128, NFC], F32, kind="ExternalInput").ap()
    d_wo = nc.dram_tensor("wo", [D, D], F32, kind="ExternalInput").ap()
    d_b2b = nc.dram_tensor("b2b", [E + 1, D], F32, kind="ExternalInput").ap()
    d_rout = nc.dram_tensor("rout", [D, E], F32, kind="ExternalInput").ap()
    d_w1 = nc.dram_tensor("w1r", [E, NF1C, 128, NFC, 128], F32, kind="ExternalInput").ap()
    d_b1c = nc.dram_tensor("b1c", [128, NF1C, E], F32, kind="ExternalInput").ap()
    d_w2 = nc.dram_tensor("w2r", [E, NF1C, 128, D], F32, kind="ExternalInput").ap()
    d_iden = nc.dram_tensor("iden", [128, 128], F32, kind="ExternalInput").ap()
    d_out = nc.dram_tensor("out", [T, D], F32, kind="ExternalOutput").ap()

    A = nc.alloc_sbuf_tensor
    xnF = A("xnF", [128, NFC * T], BF16).ap()
    optF = A("optF", [128, NFC * T], BF16).ap()
    vg = A("vg", [128, 32768], BF16).ap()
    scr = A("scr", [128, 8192], F32).ap()
    xsq = scr[:, 6144:7168]
    xst = scr[:, 2048:6144]
    mixed = scr

    wvw = A("wvw", [128, 8192], BF16).ap()
    w1sl = wvw[:, 0:4096]
    w2sl = wvw[:, 4096:8192]
    wvf_sl = wvw[:, 4096:8192]
    wo_sb = A("wo_sb", [128, NFC * 1024], BF16).ap()
    x_rot = A("x_rot", [128, 2 * 1024], F32).ap()
    xnt = scr[:, 0:2048]
    wmapE = A("wmapE", [16, T], BF16).ap()
    b2b_sb = A("b2b_sb", [16, D], BF16).ap()
    rout_sb = A("rout_sb", [128, NFC * E], BF16).ap()
    b1c_sb = A("b1c_sb", [128, NF1C * E], F32).ap()
    bvc_sb = A("bvc_sb", [128, NFC], F32).ap()
    iden_f = scr[:, 7168:7296]
    wmap_sb = A("wmap_sb", [128, NTT * E], F32).ap()
    st = scr[:, 7904:7920]
    tsc = scr[:, 7872:7904]

    if dbg:
        d_dbg_xnF = nc.dram_tensor("dbg_xnF", [128, NFC * T], F32, kind="ExternalOutput").ap()
        d_dbg_optF = nc.dram_tensor("dbg_optF", [128, NFC * T], F32, kind="ExternalOutput").ap()
        d_dbg_wmap = nc.dram_tensor("dbg_wmap", [128, NTT * E], F32, kind="ExternalOutput").ap()


    P01 = nc.alloc_psum_tensor("P01", [128, 1024], F32).ap()
    P23 = nc.alloc_psum_tensor("P23", [128, 1024], F32).ap()
    P45 = nc.alloc_psum_tensor("P45", [128, 1024], F32).ap()
    P67 = nc.alloc_psum_tensor("P67", [128, 1024], F32).ap()
    SP = [P01, P23]
    W2B = [P01[:, 0:512], P01[:, 512:1024], P23[:, 0:512], P23[:, 512:1024],
           P45[:, 0:512], P45[:, 512:1024], P67[:, 0:512], P67[:, 512:1024]]

    p = Plan(nc)
    p.mksem("pe", "act", "dve", "dx0", "dx1", "dq0", "dq1", "dq2", "dq3", "dw1a", "dw1b",
            "dw2a", "dw2b", "dw2c", "dw2d", "dw1c", "dw1d", "dcst", "dwv",
            "dwo", "do0", "do1", "ddbg", "dxp")
    DX = ("dx0", "dx1")
    DQ = ("dq0", "dq1", "dq2", "dq3")
    DW1 = ("dw1a", "dw1b", "dw1c", "dw1d")
    DW2 = ("dw2a", "dw2b", "dw2c", "dw2d")
    DO = ("do0", "do1")

    for fn in (
        lambda g: g.dma_start(iden_f, d_iden),
        lambda g: g.dma_start(b2b_sb[0:E + 1, :], d_b2b),
        lambda g: g.dma_start(rout_sb.rearrange("p (c e) -> p c e", c=NFC),
                              d_rout.rearrange("(c p) e -> p c e", p=128)),
        lambda g: g.dma_start(bvc_sb, d_bvc),
        lambda g: g.dma_start(b1c_sb.rearrange("p (c e) -> p c e", c=NF1C), d_b1c),
    ):
        p.do("gpsimd", fn, inc="dcst", inc_by=16)
    n_cst = p.cnt["dcst"]
    for fc in range(NFC):
        p.do("gpsimd", lambda g, fc=fc: g.dma_start(
            wo_sb[:, fc * 1024:(fc + 1) * 1024], d_wo[fc * 128:(fc + 1) * 128, :]),
            inc="dwo", inc_by=16)

    p.do("vector", lambda v: v.memset(wmapE, 1.0))
    p.do("vector", lambda v: v.memset(st[:, 15:16], EPS))
    eps_col = st[:, 15:16]

    DO_B = "B" in phases
    DO_C = "C" in phases
    tok_vev = {}
    tok_vproj = {"last": None}

    pair_last = {0: None, 1: None}

    def wait_pair(pidx):
        if pair_last[pidx] is not None:
            sem, tok = pair_last[pidx]
            p.wait("tensor", sem, tok)

    def set_pair(pidx, sem, tok):
        pair_last[pidx] = (sem, tok)

    nqv = {0: 0, 1: 0, 2: 0, 3: 0}
    tok_vpmm = {}

    load_ord = {}

    def issue_vp_load(ts, vc):
        sl = vc % 4
        prev = (ts, vc - 4) if vc >= 4 else (
            (ts - 1, vc + 4) if ts >= 1 else None)
        if prev is not None:
            p.wait("gpsimd", "pe", tok_vpmm[prev])
        nqv[sl] += 1
        load_ord[(ts, vc)] = nqv[sl]
        p.do("gpsimd", lambda g, vc=vc, sl=sl: g.dma_start(
            wvf_sl[:, sl * 1024:(sl + 1) * 1024].rearrange(
                "p (c n) -> p c n", c=NFC),
            d_wvf[vc].rearrange("c p n -> p c n")), inc=DQ[sl], inc_by=16)

    def emit_vp_unit(ts, vc):
        idx = 8 * ts + vc
        pair = SP[idx % 2]
        sl = vc % 4
        if (ts, vc) not in load_ord:
            issue_vp_load(ts, vc)
        wait_pair(idx % 2)
        p.wait("tensor", DQ[sl], 16 * load_ord[(ts, vc)])
        p.wait("tensor", "dve", tok_xnFev[4 * ts + 3])
        vps = pair[:, 0:512]
        for fc in range(NFC):
            tok = p.do("tensor", lambda t, fc=fc, sl=sl, ts=ts, vps=vps:
                       t.matmul(
                vps,
                wvf_sl[:, sl * 1024 + fc * 128: sl * 1024 + (fc + 1) * 128],
                xnF[:, fc * T + ts * 512: fc * T + (ts + 1) * 512],
                start=(fc == 0), stop=(fc == NFC - 1)),
                inc=("pe" if fc == NFC - 1 else None))
        tok_vpmm[(ts, vc)] = tok
        if idx + 4 < 32:
            issue_vp_load((idx + 4) // 8, (idx + 4) % 8)
        p.wait("scalar", "pe", tok)
        tok_vev[idx] = p.do("scalar", lambda s, vc=vc, ts=ts, vps=vps:
                            s.activation(
            optF[:, vc * T + ts * 512: vc * T + (ts + 1) * 512],
            vps, AF.Identity, bias=bvc_sb[:, vc:vc + 1]), inc="act")
        set_pair(idx % 2, "act", tok_vev[idx])

    tok_xnt = {}
    tok_Lcopy = {}
    tok_top2 = {}
    tok_tr = {}
    tok_xnFev = {}
    tok_wmapTev = {}
    tok_vevac = {}
    tok_vproj = {}
    nx = {0: 0, 1: 0}

    def emit_A(tt, pump, stats=None):
        par = tt % 2
        if stats is None:
            if tt >= 2:
                p.wait("sync", "dve", tok_xnt[tt - 2])
            nx[par] += 1
            p.do("sync", lambda sy, tt=tt, par=par: sy.dma_start(
                x_rot[:, par * 1024:(par + 1) * 1024],
                d_x[tt * 128:(tt + 1) * 128, :]), inc=DX[par], inc_by=16)
            xt = x_rot[:, par * 1024:(par + 1) * 1024]

            p.wait("vector", DX[par], 16 * nx[par])
            if tt >= 2:
                p.wait("vector", "pe", tok_tr[tt - 2])
            p.do("vector", lambda v, xt=xt: v.reduce_sum(st[:, 0:1], xt, axis=AX))
            p.do("vector", lambda v: v.tensor_scalar_mul(st[:, 1:2], st[:, 0:1],
                                                         1.0 / D))
            p.do("vector", lambda v, xt=xt: v.scalar_tensor_tensor(
                xsq, xt, 1.0, xt, ALU.bypass, ALU.mult, accum_out=st[:, 2:3]))
            p.do("vector", lambda v: v.tensor_scalar_mul(st[:, 3:4], st[:, 2:3],
                                                         1.0 / D))
            tok_var = p.do("vector", lambda v: v.scalar_tensor_tensor(
                st[:, 4:5], st[:, 1:2], st[:, 1:2], st[:, 3:4],
                ALU.mult, ALU.subtract), inc="dve")
            p.wait("scalar", "dve", tok_var)
            p.do("scalar", lambda s: s.activation(st[:, 5:6], st[:, 4:5], AF.Ln,
                                                  scale=-1.0, bias=eps_col))
            tok_rstd = p.do("scalar", lambda s: s.activation(
                st[:, 6:7], st[:, 5:6], AF.Exp, scale=-0.5), inc="act")
            mcol, rcol = st[:, 1:2], st[:, 6:7]
        else:
            xt, mcol, rcol, tok_rstd = stats
            if tt >= 2:
                p.wait("vector", "pe", tok_tr[tt - 2])
        xnt_t = xnt[:, par * 1024:(par + 1) * 1024]
        p.wait("vector", "act", tok_rstd)
        tok_xnt[tt] = p.do("vector", lambda v, xt=xt, xnt_t=xnt_t, mcol=mcol,
                           rcol=rcol: v.tensor_scalar(
            xnt_t, xt, mcol, rcol, ALU.subtract, ALU.mult), inc="dve")

        pump()
        p.wait("tensor", "dcst", n_cst)
        p.wait("tensor", "dve", tok_xnt[tt])
        if tt >= 1:
            p.wait("tensor", "dve", tok_xnFev[tt - 1])
        for fc in range(NFC):
            tok = p.do("tensor", lambda t, fc=fc, xnt_t=xnt_t: t.transpose(
                P67[:, fc * 128:(fc + 1) * 128], xnt_t[:, fc * 128:(fc + 1) * 128],
                iden_f), inc=("pe" if fc == NFC - 1 else None))
        tok_trans = tok
        tok_tr[tt] = tok_trans
        p.wait("vector", "pe", tok_trans)
        tok_xnFev[tt] = p.do("vector", lambda v, tt=tt: v.tensor_copy(
            xnF.rearrange("p (c t) -> p c t", c=NFC)[:, :, tt * 128:(tt + 1) * 128],
            P67.rearrange("p (c t) -> p c t", c=NFC)), inc="dve")

        pump()
        logit_ps = P45[:, 0:8]
        if tt >= 1:
            p.wait("tensor", "dve", tok_Lcopy[tt - 1])
        p.wait("tensor", "dve", tok_xnFev[tt])
        for fc in range(NFC):
            tok = p.do("tensor", lambda t, fc=fc, tt=tt: t.matmul(
                logit_ps, xnF[:, fc * T + tt * 128: fc * T + (tt + 1) * 128],
                rout_sb[:, fc * E:(fc + 1) * E],
                start=(fc == 0), stop=(fc == NFC - 1)),
                inc=("pe" if fc == NFC - 1 else None))
        p.wait("vector", "pe", tok)
        tok_Lcopy[tt] = p.do("vector", lambda v, tt=tt: v.tensor_copy(
            wmap_sb[:, tt * E:(tt + 1) * E], logit_ps), inc="dve")
        tok_p1box[0] = emit_top2_p1(tt)

    dcols = scr[:, 2048:2064]
    ecols = scr[:, 2064:2080]

    def top2_m1(tt):
        return scr[:, 2560 + tt * 8: 2568 + tt * 8]

    def top2_m2(tt):
        return scr[:, 2688 + tt * 8: 2696 + tt * 8]

    tok_p1box = [0]

    def emit_top2_p1(tt):
        L = wmap_sb[:, tt * E:(tt + 1) * E]
        m1, m2, L2 = top2_m1(tt), top2_m2(tt), tsc[:, 16:24]
        p.do("vector", lambda v, L=L: v.reduce_max(st[:, 7:8], L, axis=AX))
        p.do("vector", lambda v, L=L, m1=m1: v.tensor_single_scalar(
            m1, L, st[:, 7:8], ALU.is_equal))
        p.do("vector", lambda v, L=L, m1=m1, L2=L2: v.scalar_tensor_tensor(
            L2, m1, -1e9, L, ALU.mult, ALU.add))
        p.do("vector", lambda v, L2=L2: v.reduce_max(st[:, 8:9], L2, axis=AX))
        p.do("vector", lambda v, L2=L2, m2=m2: v.tensor_single_scalar(
            m2, L2, st[:, 8:9], ALU.is_equal))
        return p.do("vector", lambda v, tt=tt: v.tensor_sub(
            dcols[:, tt:tt + 1], st[:, 8:9], st[:, 7:8]), inc="dve")

    def emit_top2_p3(tt):
        L = wmap_sb[:, tt * E:(tt + 1) * E]
        m1, m2 = top2_m1(tt), top2_m2(tt)
        p.do("vector", lambda v, tt=tt: v.tensor_scalar_add(
            st[:, 11:12], ecols[:, tt:tt + 1], 1.0))
        p.do("vector", lambda v: v.reciprocal(st[:, 12:13], st[:, 11:12]))
        p.do("vector", lambda v, tt=tt: v.tensor_mul(
            st[:, 13:14], ecols[:, tt:tt + 1], st[:, 12:13]))
        p.do("vector", lambda v, m2=m2: v.tensor_scalar_mul(m2, m2, st[:, 13:14]))
        tok_top2[tt] = p.do("vector", lambda v, m1=m1, m2=m2, L=L:
                            v.scalar_tensor_tensor(
            L, m1, st[:, 12:13], m2, ALU.mult, ALU.add), inc="dve")

    def emit_top2_wt(tt):
        L = wmap_sb[:, tt * E:(tt + 1) * E]
        wmT_ps = P45[0:E, 128 + (tt % 2) * 384:256 + (tt % 2) * 384]
        p.wait("tensor", "dve", tok_top2[tt])
        if tt >= 2:
            p.wait("tensor", "dve", tok_wmapTev[tt - 2])
        tok_wt = p.do("tensor", lambda t, L=L, wmT_ps=wmT_ps:
                      t.transpose(wmT_ps, L, iden_f), inc="pe")
        p.wait("vector", "pe", tok_wt)
        tok_wmapTev[tt] = p.do("vector", lambda v, tt=tt, wmT_ps=wmT_ps:
                               v.tensor_copy(
            wmapE[0:E, tt * 128:(tt + 1) * 128], wmT_ps), inc="dve")

    nw1 = {0: 0, 1: 0, 2: 0, 3: 0}
    nw2 = {0: 0, 1: 0, 2: 0, 3: 0}
    tok_w1use = {}
    tok_w2use = {}
    tok_oev = {}
    tok_gelu = {}
    tok_mix = {}
    tok_fadd = {}
    nst = {0: 0, 1: 0}
    last_sweep = [None]
    tok_w2_reader = {}

    def emit_h1_group(half, e, g):
        for f1c in (2 * g, 2 * g + 1):
            sl = f1c % 4
            if tok_w1use.get(sl) is not None:
                p.wait("gpsimd", "pe", tok_w1use[sl])
            nw1[sl] += 1
            p.do("gpsimd", lambda gg, e=e, f1c=f1c, sl=sl: gg.dma_start(
                w1sl[:, sl * 1024:(sl + 1) * 1024],
                d_w1[e, f1c].rearrange("p f n -> p (f n)")),
                inc=DW1[sl], inc_by=16)
        for f1c in (2 * g, 2 * g + 1):
            sl = f1c % 4
            pair = SP[f1c % 2]
            p.wait("tensor", DW1[sl], 16 * nw1[sl])
            wait_pair(f1c % 2)
            if e == 0 and half == 0:
                p.wait("tensor", "dve", tok_xnFev[7])
            if tok_w2_reader.get(f1c) is not None:
                p.wait("tensor", "pe", tok_w2_reader[f1c])
            for j in range(2):
                for fc in range(NFC):
                    tok = p.do("tensor",
                               lambda t, fc=fc, f1c=f1c, j=j, sl=sl, half=half,
                               pair=pair: t.matmul(
                                   pair[:, j * 512:(j + 1) * 512],
                                   w1sl[:, sl * 1024 + fc * 128:
                                        sl * 1024 + fc * 128 + 128],
                                   xnF[:, fc * T + half * 1024 + j * 512:
                                       fc * T + half * 1024 + (j + 1) * 512],
                                   start=(fc == 0), stop=(fc == NFC - 1)),
                               inc=("pe" if (j == 1 and fc == NFC - 1) else None))
            tok_w1use[sl] = tok
            p.wait("scalar", "pe", tok)
            tok_gelu[(half, e, f1c)] = p.do(
                "scalar", lambda s, f1c=f1c, e=e, pair=pair: s.activation(
                    vg[:, f1c * 1024:(f1c + 1) * 1024], pair, AF.Gelu,
                    bias=b1c_sb[:, f1c * E + e: f1c * E + e + 1]), inc="act")
            set_pair(f1c % 2, "act", tok_gelu[(half, e, f1c)])

    def emit_h1(half, e):
        for g in range(NF1C // 2):
            emit_h1_group(half, e, g)

    def emit_outproj(half):
        p.wait("tensor", "dwo", 16 * NFC)
        for tl in range(8):
            tt = half * 8 + tl
            pair = SP[tl % 2]
            wait_pair(tl % 2)
            if tl < 2:
                p.wait("tensor", "act", tok_optF_all)
            for fc in range(NFC):
                for j in range(2):
                    p.do("tensor", lambda t, fc=fc, tt=tt, j=j, pair=pair: t.matmul(
                        pair[:, j * 512:(j + 1) * 512],
                        optF[:, fc * T + tt * 128: fc * T + (tt + 1) * 128],
                        wo_sb[:, fc * 1024 + j * 512: fc * 1024 + (j + 1) * 512],
                        start=(fc == 0), stop=False))
            for j in range(2):
                tok = p.do("tensor", lambda t, tt=tt, j=j, pair=pair: t.matmul(
                    pair[:, j * 512:(j + 1) * 512],
                    wmapE[0:E + 1, tt * 128:(tt + 1) * 128],
                    b2b_sb[0:E + 1, j * 512:(j + 1) * 512],
                    start=False, stop=True), inc=("pe" if j == 1 else None))
            p.wait("vector", "pe", tok)
            tok_oev[(half, tl)] = p.do("vector", lambda v, tl=tl, pair=pair:
                                       v.tensor_copy(
                mixed[:, tl * 1024:(tl + 1) * 1024], pair), inc="dve")
            set_pair(tl % 2, "dve", tok_oev[(half, tl)])

    def emit_w2(half, e):
        for hh in range(2):
            sweep = (half, e, hh)
            for g in range(NF1C // 2):
                sl = g % 4
                if tok_w2use.get(sl) is not None:
                    p.wait("gpsimd", "pe", tok_w2use[sl])
                elif DO_B:
                    p.wait("gpsimd", "pe", tok_vpmm[(3, 4 + sl)])
                nw2[sl] += 1
                p.do("gpsimd", lambda gg, e=e, g=g, hh=hh, sl=sl: gg.dma_start(
                    w2sl[:, sl * 1024:(sl + 1) * 1024].rearrange(
                        "p (c n) -> p c n", c=2),
                    d_w2[e, 2 * g:2 * g + 2, :, hh * 512:(hh + 1) * 512].rearrange(
                        "c p n -> p c n")),
                    inc=DW2[sl], inc_by=16)
                for f1c in (2 * g, 2 * g + 1):
                    p.wait("tensor", DW2[sl], 16 * nw2[sl])
                    p.wait("tensor", "act", tok_gelu[(half, e, f1c)])
                    for tl in range(8):
                        if f1c == 0:
                            if tl < 4:
                                wait_pair(tl // 2)
                            if last_sweep[0] is not None and \
                                    (last_sweep[0], tl) in tok_mix:
                                p.wait("tensor", "dve",
                                       tok_mix[(last_sweep[0], tl)])
                            if hh == 0 and tl >= 4 and half == 0 and e == 0:
                                p.wait("tensor", "dve", tok_wmap_all)
                                p.wait("tensor", "dve", tok_xnF_all)
                        want_inc = (f1c == NF1C - 1) or (f1c % 2 == 1 and tl == 7)
                        tok = p.do("tensor",
                                   lambda t, f1c=f1c, tl=tl, sl=sl, hh=hh:
                                   t.matmul(
                                       W2B[tl],
                                       vg[:, f1c * 1024 + tl * 128:
                                          f1c * 1024 + (tl + 1) * 128],
                                       w2sl[:, sl * 1024 + (f1c % 2) * 512:
                                            sl * 1024 + (f1c % 2) * 512 + 512],
                                       start=(f1c == 0), stop=(f1c == NF1C - 1)),
                                   inc=("pe" if want_inc else None))
                        if f1c == NF1C - 1:
                            tt = half * 8 + tl
                            p.wait("vector", "pe", tok)
                            tok_mix[(sweep, tl)] = p.do(
                                "vector",
                                lambda v, tl=tl, hh=hh, tt=tt, e=e:
                                v.scalar_tensor_tensor(
                                    mixed[:, tl * 1024 + hh * 512:
                                          tl * 1024 + hh * 512 + 512],
                                    W2B[tl],
                                    wmap_sb[:, tt * E + e: tt * E + e + 1],
                                    mixed[:, tl * 1024 + hh * 512:
                                          tl * 1024 + hh * 512 + 512],
                                    ALU.mult, ALU.add), inc="dve")
                            if tl < 4:
                                set_pair(tl // 2, "dve", tok_mix[(sweep, tl)])
                        tok_w2use[sl] = tok
                    tok_w2_reader[f1c] = tok
            last_sweep[0] = sweep

    def emit_final(half):
        for tl in range(8):
            tt = half * 8 + tl
            par = tt % 2
            if nst[par] > 0:
                p.wait("sync", DO[par], 16 * nst[par])
            p.wait("sync", "dve", tok_xnt[NTT - 2 + par])
            if tt >= 2 and (tt - 2) in tok_fadd:
                p.wait("sync", "dve", tok_fadd[tt - 2])
            nx[par] += 1
            p.do("sync", lambda sy, tt=tt, par=par: sy.dma_start(
                x_rot[:, par * 1024:(par + 1) * 1024],
                d_x[tt * 128:(tt + 1) * 128, :]), inc=DX[par], inc_by=16)
            p.wait("vector", DX[par], 16 * nx[par])
            for hh in range(2):
                p.wait("vector", "dve", tok_mix[((half, E - 1, hh), tl)])
            tok_fadd[tt] = p.do("vector", lambda v, par=par, tl=tl: v.tensor_add(
                x_rot[:, par * 1024:(par + 1) * 1024],
                mixed[:, tl * 1024:(tl + 1) * 1024],
                x_rot[:, par * 1024:(par + 1) * 1024]), inc="dve")
            p.wait("sync", "dve", tok_fadd[tt])
            nst[par] += 1
            p.do("sync", lambda sy, tt=tt, par=par: sy.dma_start(
                d_out[tt * 128:(tt + 1) * 128, :],
                x_rot[:, par * 1024:(par + 1) * 1024]), inc=DO[par], inc_by=16)

    vpq = []
    h1q = []
    last_kind = ["h1"]

    def pump():
        a = vpq if (last_kind[0] == "h1" and vpq) or not h1q else h1q
        if a:
            last_kind[0] = "vp" if a is vpq else "h1"
            a.pop(0)()

    for t4 in range(4):
        p.do("sync", lambda sy, t4=t4: sy.dma_start(
            xst[:, t4 * 1024:(t4 + 1) * 1024], d_x[t4 * 128:(t4 + 1) * 128, :]),
            inc="dxp", inc_by=16)
    p.wait("vector", "dxp", 64)
    for t4 in range(4):
        xt4 = xst[:, t4 * 1024:(t4 + 1) * 1024]
        p.do("vector", lambda v, xt4=xt4: v.reduce_sum(st[:, 0:1], xt4, axis=AX))
        p.do("vector", lambda v, t4=t4: v.tensor_scalar_mul(
            tsc[:, t4:t4 + 1], st[:, 0:1], 1.0 / D))
        p.do("vector", lambda v, xt4=xt4: v.scalar_tensor_tensor(
            xsq, xt4, 1.0, xt4, ALU.bypass, ALU.mult, accum_out=st[:, 2:3]))
        p.do("vector", lambda v: v.tensor_scalar_mul(st[:, 3:4], st[:, 2:3],
                                                     1.0 / D))
        p.do("vector", lambda v, t4=t4: v.scalar_tensor_tensor(
            tsc[:, 4 + t4:5 + t4], tsc[:, t4:t4 + 1], tsc[:, t4:t4 + 1],
            st[:, 3:4], ALU.mult, ALU.subtract), inc=(None if t4 < 3 else "dve"))
    tok_bvar = p.cnt["dve"]
    p.wait("scalar", "dve", tok_bvar)
    p.do("scalar", lambda s: s.activation(tsc[:, 12:16], tsc[:, 4:8], AF.Ln,
                                          scale=-1.0, bias=eps_col))
    tok_brstd = p.do("scalar", lambda s: s.activation(
        tsc[:, 8:12], tsc[:, 12:16], AF.Exp, scale=-0.5), inc="act")

    for tt in range(NTT):
        if tt < 4:
            emit_A(tt, pump, stats=(xst[:, tt * 1024:(tt + 1) * 1024],
                                    tsc[:, tt:tt + 1], tsc[:, 8 + tt:9 + tt],
                                    tok_brstd))
        else:
            emit_A(tt, pump)
        if DO_B and tt % 4 == 3:
            ts = tt // 4
            if ts == 0:
                for vc in range(4):
                    issue_vp_load(0, vc)
            vpq.extend([(lambda ts=ts, vc=vc: emit_vp_unit(ts, vc))
                        for vc in range(NFC)])
        if DO_C and tt == 7:
            h1q.extend([(lambda g=g: emit_h1_group(0, 0, g))
                        for g in range(NF1C // 2)])
    p.wait("scalar", "dve", tok_p1box[0])
    tok_te = p.do("scalar", lambda s: s.activation(ecols, dcols, AF.Exp),
                  inc="act")
    p.wait("vector", "act", tok_te)
    for tt in range(NTT):
        emit_top2_p3(tt)
    while vpq or h1q:
        pump()
    for tt in range(NTT):
        emit_top2_wt(tt)
    tok_xnF_all = tok_xnFev[NTT - 1]
    tok_wmap_all = tok_wmapTev[NTT - 1]
    if DO_B:
        tok_optF_all = tok_vev[8 * 4 - 1]
    for half in (range(2) if DO_C else []):
        if half == 1:
            emit_h1(1, 0)
        emit_outproj(half)
        emit_w2(half, 0)
        for e in range(1, E):
            emit_h1(half, e)
            emit_w2(half, e)
        emit_final(half)

    if dbg:
        p.wait("gpsimd", "dve", p.cnt["dve"])
        p.wait("gpsimd", "act", p.cnt["act"])
        p.wait("gpsimd", "pe", p.cnt["pe"])
        dumps = [(d_dbg_wmap, wmap_sb), (d_dbg_xnF, xnF)]
        if DO_B:
            dumps.append((d_dbg_optF, optF))
        for dst, s_ in dumps:
            p.do("gpsimd", lambda g, dst=dst, s_=s_: g.dma_start(dst, s_),
                 inc="ddbg", inc_by=16)
        p.wait("gpsimd", "ddbg", p.cnt["ddbg"])

    p.wait("sync", DO[0], 16 * nst[0])
    p.wait("sync", DO[1], 16 * nst[1])

    with nc.Block() as blk:
        @blk.tensor
        def _(En):
            for f in p.streams["tensor"]:
                f(En)

        @blk.scalar
        def _(En):
            for f in p.streams["scalar"]:
                f(En)

        @blk.vector
        def _(En):
            for f in p.streams["vector"]:
                f(En)

        @blk.gpsimd
        def _(En):
            for f in p.streams["gpsimd"]:
                f(En)

        @blk.sync
        def _(En):
            for f in p.streams["sync"]:
                f(En)

    return nc


_CACHE = {}


def _prep(inputs):
    w_in = np.asarray(inputs["L0_attn_w_in"], np.float32)
    b_in = np.asarray(inputs["L0_attn_b_in"], np.float32)
    w_out = np.asarray(inputs["L0_attn_w_out"], np.float32)
    b_out = np.asarray(inputs["L0_attn_b_out"], np.float32)
    rw = np.asarray(inputs["L0_ffn_router_w"], np.float32)
    w1 = np.asarray(inputs["L0_ffn_w1"], np.float32)
    b1 = np.asarray(inputs["L0_ffn_b1"], np.float32)
    w2 = np.asarray(inputs["L0_ffn_w2"], np.float32)
    b2 = np.asarray(inputs["L0_ffn_b2"], np.float32)

    Ar = w_in.reshape(NH, 192, D)
    wvf2 = Ar[:, 128:192, :].reshape(D, D).T
    wvf = np.ascontiguousarray(
        wvf2.reshape(NFC, 128, NFC, 128).transpose(2, 0, 1, 3))
    bvc = np.ascontiguousarray(
        b_in.reshape(NH, 192)[:, 128:192].reshape(NFC, 128).T)
    wo = np.ascontiguousarray(w_out.T)
    b2b = np.ascontiguousarray(np.concatenate([b2, b_out[None, :]], axis=0))
    rout = np.ascontiguousarray(rw.T)
    w1r = np.ascontiguousarray(
        w1.reshape(E, NF1C, 128, NFC, 128).transpose(0, 1, 4, 3, 2))
    b1c = np.ascontiguousarray(b1.reshape(E, NF1C, 128).transpose(2, 1, 0))
    w2r = np.ascontiguousarray(w2.reshape(E, D, NF1C, 128).transpose(0, 2, 3, 1))
    iden = np.eye(128, dtype=np.float32)
    return dict(wvf=wvf, bvc=bvc, wo=wo, b2b=b2b, rout=rout,
                w1r=w1r, b1c=b1c, w2r=w2r, iden=iden)


def kernel(**inputs):
    x = np.ascontiguousarray(np.asarray(inputs["L0_attn_x"], np.float32))
    w = _prep(inputs)
    if "nc" not in _CACHE:
        _CACHE["nc"] = build_nc()
    nc = _CACHE["nc"]
    in_maps = [dict(x=np.ascontiguousarray(x[c]), **w) for c in range(8)]
    res = run_bass_kernel_spmd(nc, in_maps, core_ids=list(range(8)))
    return np.stack([np.asarray(res.results[c]["out"]) for c in range(8)], axis=0)


# version 15
# speedup vs baseline: 1.0240x; 1.0240x over previous
import numpy as np
import concourse.bass as bass
import concourse.mybir as mybir
from concourse.bass_utils import run_bass_kernel_spmd

F32 = mybir.dt.float32
BF16 = mybir.dt.bfloat16
AF = mybir.ActivationFunctionType
ALU = mybir.AluOpType
AX = mybir.AxisListType.X

T = 2048
D = 1024
NH = 16
DH = 64
E = 8
F = 4096
NFC = 8
NF1C = 32
NTT = 16
EPS = 1e-5

ENGINES = ("tensor", "scalar", "vector", "gpsimd", "sync")


class Plan:

    def __init__(self, nc):
        self.nc = nc
        self.streams = {e: [] for e in ENGINES}
        self.sem = {}
        self.cnt = {}
        self.waited = {e: {} for e in ENGINES}

    def mksem(self, *names):
        for n in names:
            self.sem[n] = self.nc.alloc_semaphore(n)
            self.cnt[n] = 0

    def wait(self, eng, name, val=None):
        v = self.cnt[name] if val is None else val
        if v <= 0 or self.waited[eng].get(name, -1) >= v:
            return
        self.waited[eng][name] = v
        s = self.sem[name]
        self.streams[eng].append(lambda En, s=s, v=v: En.wait_ge(s, v))

    SELF_SEM = {"vector": "dve", "scalar": "act"}

    def do(self, eng, fn, inc=None, inc_by=1):
        own = self.SELF_SEM.get(eng)
        if own is not None:
            self.wait(eng, own)
            if inc is None:
                inc, inc_by = own, 1
        if inc is None:
            self.streams[eng].append(fn)
            return None
        self.cnt[inc] += inc_by
        tok = self.cnt[inc]
        s = self.sem[inc]
        self.streams[eng].append(
            lambda En, fn=fn, s=s, n=inc_by: fn(En).then_inc(s, n)
        )
        return tok


def build_nc(phases="ABC", dbg=False):
    nc = bass.Bass("TRN2", target_bir_lowering=False)

    d_x = nc.dram_tensor("x", [T, D], F32, kind="ExternalInput").ap()
    d_wvf = nc.dram_tensor("wvf", [NFC, NFC, 128, 128], F32, kind="ExternalInput").ap()
    d_bvc = nc.dram_tensor("bvc", [128, NFC], F32, kind="ExternalInput").ap()
    d_wo = nc.dram_tensor("wo", [D, D], F32, kind="ExternalInput").ap()
    d_b2b = nc.dram_tensor("b2b", [E + 1, D], F32, kind="ExternalInput").ap()
    d_rout = nc.dram_tensor("rout", [D, E], F32, kind="ExternalInput").ap()
    d_w1 = nc.dram_tensor("w1r", [E, NF1C, 128, NFC, 128], F32, kind="ExternalInput").ap()
    d_b1c = nc.dram_tensor("b1c", [128, NF1C, E], F32, kind="ExternalInput").ap()
    d_w2 = nc.dram_tensor("w2r", [E, NF1C, 128, D], F32, kind="ExternalInput").ap()
    d_iden = nc.dram_tensor("iden", [128, 128], F32, kind="ExternalInput").ap()
    d_out = nc.dram_tensor("out", [T, D], F32, kind="ExternalOutput").ap()

    A = nc.alloc_sbuf_tensor
    xnF = A("xnF", [128, NFC * T], BF16).ap()
    optF = A("optF", [128, NFC * T], BF16).ap()
    vg = A("vg", [128, 32768], BF16).ap()
    scr = A("scr", [128, 8192], F32).ap()
    xsq = scr[:, 6144:7168]
    xst = scr[:, 2048:6144]
    mixed = scr

    wvw = A("wvw", [128, 8192], BF16).ap()
    w1sl = wvw[:, 0:4096]
    w2sl = wvw[:, 4096:8192]
    wvf_sl = wvw[:, 6144:8192]
    wo_sb = A("wo_sb", [128, NFC * 1024], BF16).ap()
    x_rot = A("x_rot", [128, 2 * 1024], F32).ap()
    xnt = scr[:, 0:2048]
    wmapE = A("wmapE", [16, T], BF16).ap()
    b2b_sb = A("b2b_sb", [16, D], BF16).ap()
    rout_sb = A("rout_sb", [128, NFC * E], BF16).ap()
    b1c_sb = A("b1c_sb", [128, NF1C * E], F32).ap()
    bvc_sb = A("bvc_sb", [128, NFC], F32).ap()
    iden_f = scr[:, 7168:7296]
    wmap_sb = A("wmap_sb", [128, NTT * E], F32).ap()
    st = scr[:, 7904:7920]
    tsc = scr[:, 7872:7904]

    if dbg:
        d_dbg_xnF = nc.dram_tensor("dbg_xnF", [128, NFC * T], F32, kind="ExternalOutput").ap()
        d_dbg_optF = nc.dram_tensor("dbg_optF", [128, NFC * T], F32, kind="ExternalOutput").ap()
        d_dbg_wmap = nc.dram_tensor("dbg_wmap", [128, NTT * E], F32, kind="ExternalOutput").ap()


    P01 = nc.alloc_psum_tensor("P01", [128, 1024], F32).ap()
    P23 = nc.alloc_psum_tensor("P23", [128, 1024], F32).ap()
    P45 = nc.alloc_psum_tensor("P45", [128, 1024], F32).ap()
    P67 = nc.alloc_psum_tensor("P67", [128, 1024], F32).ap()
    SP = [P01, P23]
    W2B = [P01[:, 0:512], P01[:, 512:1024], P23[:, 0:512], P23[:, 512:1024],
           P45[:, 0:512], P45[:, 512:1024], P67[:, 0:512], P67[:, 512:1024]]

    p = Plan(nc)
    p.mksem("pe", "act", "dve", "dx0", "dx1", "dq0", "dq1", "dw1a", "dw1b",
            "dw2a", "dw2b", "dw2c", "dw2d", "dw1c", "dw1d", "dcst", "dwv",
            "dwo", "do0", "do1", "ddbg", "dxp", "dxq")
    DX = ("dx0", "dx1")
    DQ = ("dq0", "dq1")
    DW1 = ("dw1a", "dw1b", "dw1c", "dw1d")
    DW2 = ("dw2a", "dw2b", "dw2c", "dw2d")
    DO = ("do0", "do1")

    for fn in (
        lambda g: g.dma_start(iden_f, d_iden),
        lambda g: g.dma_start(b2b_sb[0:E + 1, :], d_b2b),
        lambda g: g.dma_start(rout_sb.rearrange("p (c e) -> p c e", c=NFC),
                              d_rout.rearrange("(c p) e -> p c e", p=128)),
        lambda g: g.dma_start(bvc_sb, d_bvc),
        lambda g: g.dma_start(b1c_sb.rearrange("p (c e) -> p c e", c=NF1C), d_b1c),
    ):
        p.do("gpsimd", fn, inc="dcst", inc_by=16)
    n_cst = p.cnt["dcst"]
    for fc in range(NFC):
        p.do("gpsimd", lambda g, fc=fc: g.dma_start(
            wo_sb[:, fc * 1024:(fc + 1) * 1024], d_wo[fc * 128:(fc + 1) * 128, :]),
            inc="dwo", inc_by=16)

    p.do("vector", lambda v: v.memset(wmapE, 1.0))
    p.do("vector", lambda v: v.memset(st[:, 15:16], EPS))
    eps_col = st[:, 15:16]

    DO_B = "B" in phases
    DO_C = "C" in phases
    tok_vev = {}
    tok_vproj = {"last": None}

    pair_last = {0: None, 1: None}

    def wait_pair(pidx):
        if pair_last[pidx] is not None:
            sem, tok = pair_last[pidx]
            p.wait("tensor", sem, tok)

    def set_pair(pidx, sem, tok):
        pair_last[pidx] = (sem, tok)

    nqv = {0: 0, 1: 0}
    tok_vpmm = {}

    def emit_vp_unit(ts, vc):
        idx = 8 * ts + vc
        pair = SP[idx % 2]
        sl = vc % 2
        prev = (ts, vc - 2) if vc >= 2 else ((ts - 1, 6 + vc) if ts >= 1 else None)
        if prev is not None:
            p.wait("gpsimd", "pe", tok_vpmm[prev])
        nqv[sl] += 1
        p.do("gpsimd", lambda g, vc=vc, sl=sl: g.dma_start(
            wvf_sl[:, sl * 1024:(sl + 1) * 1024].rearrange(
                "p (c n) -> p c n", c=NFC),
            d_wvf[vc].rearrange("c p n -> p c n")), inc=DQ[sl], inc_by=16)
        wait_pair(idx % 2)
        p.wait("tensor", DQ[sl], 16 * nqv[sl])
        p.wait("tensor", "dve", tok_xnFev[4 * ts + 3])
        vps = pair[:, 0:512]
        for fc in range(NFC):
            tok = p.do("tensor", lambda t, fc=fc, sl=sl, ts=ts, vps=vps:
                       t.matmul(
                vps,
                wvf_sl[:, sl * 1024 + fc * 128: sl * 1024 + (fc + 1) * 128],
                xnF[:, fc * T + ts * 512: fc * T + (ts + 1) * 512],
                start=(fc == 0), stop=(fc == NFC - 1)),
                inc=("pe" if fc == NFC - 1 else None))
        tok_vpmm[(ts, vc)] = tok
        p.wait("scalar", "pe", tok)
        tok_vev[idx] = p.do("scalar", lambda s, vc=vc, ts=ts, vps=vps:
                            s.activation(
            optF[:, vc * T + ts * 512: vc * T + (ts + 1) * 512],
            vps, AF.Identity, bias=bvc_sb[:, vc:vc + 1]), inc="act")
        set_pair(idx % 2, "act", tok_vev[idx])

    tok_xnt = {}
    tok_Lcopy = {}
    tok_top2 = {}
    tok_tr = {}
    tok_xnFev = {}
    tok_wmapTev = {}
    tok_vevac = {}
    tok_vproj = {}
    nx = {0: 0, 1: 0}

    def emit_A(tt, pump, stats):
        par = tt % 2
        xt, mcol, rcol, tok_rstd = stats
        if tt >= 2:
            p.wait("vector", "pe", tok_tr[tt - 2])
        xnt_t = xnt[:, par * 1024:(par + 1) * 1024]
        p.wait("vector", "act", tok_rstd)
        tok_xnt[tt] = p.do("vector", lambda v, xt=xt, xnt_t=xnt_t, mcol=mcol,
                           rcol=rcol: v.tensor_scalar(
            xnt_t, xt, mcol, rcol, ALU.subtract, ALU.mult), inc="dve")

        pump()
        p.wait("tensor", "dcst", n_cst)
        p.wait("tensor", "dve", tok_xnt[tt])
        if tt >= 1:
            p.wait("tensor", "dve", tok_xnFev[tt - 1])
        for fc in range(NFC):
            tok = p.do("tensor", lambda t, fc=fc, xnt_t=xnt_t: t.transpose(
                P67[:, fc * 128:(fc + 1) * 128], xnt_t[:, fc * 128:(fc + 1) * 128],
                iden_f), inc=("pe" if fc == NFC - 1 else None))
        tok_trans = tok
        tok_tr[tt] = tok_trans
        p.wait("vector", "pe", tok_trans)
        tok_xnFev[tt] = p.do("vector", lambda v, tt=tt: v.tensor_copy(
            xnF.rearrange("p (c t) -> p c t", c=NFC)[:, :, tt * 128:(tt + 1) * 128],
            P67.rearrange("p (c t) -> p c t", c=NFC)), inc="dve")

        pump()
        logit_ps = P45[:, 0:8]
        if tt >= 1:
            p.wait("tensor", "dve", tok_Lcopy[tt - 1])
        p.wait("tensor", "dve", tok_xnFev[tt])
        for fc in range(NFC):
            tok = p.do("tensor", lambda t, fc=fc, tt=tt: t.matmul(
                logit_ps, xnF[:, fc * T + tt * 128: fc * T + (tt + 1) * 128],
                rout_sb[:, fc * E:(fc + 1) * E],
                start=(fc == 0), stop=(fc == NFC - 1)),
                inc=("pe" if fc == NFC - 1 else None))
        p.wait("vector", "pe", tok)
        tok_Lcopy[tt] = p.do("vector", lambda v, tt=tt: v.tensor_copy(
            wmap_sb[:, tt * E:(tt + 1) * E], logit_ps), inc="dve")
        tok_p1box[0] = emit_top2_p1(tt)

    dcols = scr[:, 7296:7312]
    ecols = scr[:, 7312:7328]

    def top2_m1(tt):
        return scr[:, 7328 + tt * 8: 7336 + tt * 8]

    def top2_m2(tt):
        return scr[:, 7456 + tt * 8: 7464 + tt * 8]

    tok_p1box = [0]

    def emit_top2_p1(tt):
        L = wmap_sb[:, tt * E:(tt + 1) * E]
        m1, m2, L2 = top2_m1(tt), top2_m2(tt), tsc[:, 16:24]
        p.do("vector", lambda v, L=L: v.reduce_max(st[:, 7:8], L, axis=AX))
        p.do("vector", lambda v, L=L, m1=m1: v.tensor_single_scalar(
            m1, L, st[:, 7:8], ALU.is_equal))
        p.do("vector", lambda v, L=L, m1=m1, L2=L2: v.scalar_tensor_tensor(
            L2, m1, -1e9, L, ALU.mult, ALU.add))
        p.do("vector", lambda v, L2=L2: v.reduce_max(st[:, 8:9], L2, axis=AX))
        p.do("vector", lambda v, L2=L2, m2=m2: v.tensor_single_scalar(
            m2, L2, st[:, 8:9], ALU.is_equal))
        return p.do("vector", lambda v, tt=tt: v.tensor_sub(
            dcols[:, tt:tt + 1], st[:, 8:9], st[:, 7:8]), inc="dve")

    def emit_top2_p3(tt):
        L = wmap_sb[:, tt * E:(tt + 1) * E]
        m1, m2 = top2_m1(tt), top2_m2(tt)
        p.do("vector", lambda v, tt=tt: v.tensor_scalar_add(
            st[:, 11:12], ecols[:, tt:tt + 1], 1.0))
        p.do("vector", lambda v: v.reciprocal(st[:, 12:13], st[:, 11:12]))
        p.do("vector", lambda v, tt=tt: v.tensor_mul(
            st[:, 13:14], ecols[:, tt:tt + 1], st[:, 12:13]))
        p.do("vector", lambda v, m2=m2: v.tensor_scalar_mul(m2, m2, st[:, 13:14]))
        tok_top2[tt] = p.do("vector", lambda v, m1=m1, m2=m2, L=L:
                            v.scalar_tensor_tensor(
            L, m1, st[:, 12:13], m2, ALU.mult, ALU.add), inc="dve")

    def emit_top2_wt(tt):
        L = wmap_sb[:, tt * E:(tt + 1) * E]
        wmT_ps = P45[0:E, 128 + (tt % 2) * 384:256 + (tt % 2) * 384]
        p.wait("tensor", "dve", tok_top2[tt])
        if tt >= 2:
            p.wait("tensor", "dve", tok_wmapTev[tt - 2])
        tok_wt = p.do("tensor", lambda t, L=L, wmT_ps=wmT_ps:
                      t.transpose(wmT_ps, L, iden_f), inc="pe")
        p.wait("vector", "pe", tok_wt)
        tok_wmapTev[tt] = p.do("vector", lambda v, tt=tt, wmT_ps=wmT_ps:
                               v.tensor_copy(
            wmapE[0:E, tt * 128:(tt + 1) * 128], wmT_ps), inc="dve")

    nw1 = {0: 0, 1: 0, 2: 0, 3: 0}
    nw2 = {0: 0, 1: 0, 2: 0, 3: 0}
    tok_w1use = {}
    tok_w2use = {}
    tok_oev = {}
    tok_gelu = {}
    tok_mix = {}
    tok_fadd = {}
    nst = {0: 0, 1: 0}
    last_sweep = [None]
    tok_w2_reader = {}

    def emit_h1_group(half, e, g):
        for f1c in (2 * g, 2 * g + 1):
            sl = f1c % 4
            if tok_w1use.get(sl) is not None:
                p.wait("gpsimd", "pe", tok_w1use[sl])
            nw1[sl] += 1
            p.do("gpsimd", lambda gg, e=e, f1c=f1c, sl=sl: gg.dma_start(
                w1sl[:, sl * 1024:(sl + 1) * 1024],
                d_w1[e, f1c].rearrange("p f n -> p (f n)")),
                inc=DW1[sl], inc_by=16)
        for f1c in (2 * g, 2 * g + 1):
            sl = f1c % 4
            pair = SP[f1c % 2]
            p.wait("tensor", DW1[sl], 16 * nw1[sl])
            wait_pair(f1c % 2)
            if e == 0 and half == 0:
                p.wait("tensor", "dve", tok_xnFev[7])
            if tok_w2_reader.get(f1c) is not None:
                p.wait("tensor", "pe", tok_w2_reader[f1c])
            for j in range(2):
                for fc in range(NFC):
                    tok = p.do("tensor",
                               lambda t, fc=fc, f1c=f1c, j=j, sl=sl, half=half,
                               pair=pair: t.matmul(
                                   pair[:, j * 512:(j + 1) * 512],
                                   w1sl[:, sl * 1024 + fc * 128:
                                        sl * 1024 + fc * 128 + 128],
                                   xnF[:, fc * T + half * 1024 + j * 512:
                                       fc * T + half * 1024 + (j + 1) * 512],
                                   start=(fc == 0), stop=(fc == NFC - 1)),
                               inc=("pe" if (j == 1 and fc == NFC - 1) else None))
            tok_w1use[sl] = tok
            p.wait("scalar", "pe", tok)
            tok_gelu[(half, e, f1c)] = p.do(
                "scalar", lambda s, f1c=f1c, e=e, pair=pair: s.activation(
                    vg[:, f1c * 1024:(f1c + 1) * 1024], pair, AF.Gelu,
                    bias=b1c_sb[:, f1c * E + e: f1c * E + e + 1]), inc="act")
            set_pair(f1c % 2, "act", tok_gelu[(half, e, f1c)])

    def emit_h1(half, e):
        for g in range(NF1C // 2):
            emit_h1_group(half, e, g)

    def emit_outproj(half):
        p.wait("tensor", "dwo", 16 * NFC)
        for tl in range(8):
            tt = half * 8 + tl
            pair = SP[tl % 2]
            wait_pair(tl % 2)
            if tl < 2:
                p.wait("tensor", "act", tok_optF_all)
            for fc in range(NFC):
                for j in range(2):
                    p.do("tensor", lambda t, fc=fc, tt=tt, j=j, pair=pair: t.matmul(
                        pair[:, j * 512:(j + 1) * 512],
                        optF[:, fc * T + tt * 128: fc * T + (tt + 1) * 128],
                        wo_sb[:, fc * 1024 + j * 512: fc * 1024 + (j + 1) * 512],
                        start=(fc == 0), stop=False))
            for j in range(2):
                tok = p.do("tensor", lambda t, tt=tt, j=j, pair=pair: t.matmul(
                    pair[:, j * 512:(j + 1) * 512],
                    wmapE[0:E + 1, tt * 128:(tt + 1) * 128],
                    b2b_sb[0:E + 1, j * 512:(j + 1) * 512],
                    start=False, stop=True), inc=("pe" if j == 1 else None))
            p.wait("vector", "pe", tok)
            tok_oev[(half, tl)] = p.do("vector", lambda v, tl=tl, pair=pair:
                                       v.tensor_copy(
                mixed[:, tl * 1024:(tl + 1) * 1024], pair), inc="dve")
            set_pair(tl % 2, "dve", tok_oev[(half, tl)])

    def emit_w2(half, e):
        for hh in range(2):
            sweep = (half, e, hh)
            for g in range(NF1C // 2):
                sl = g % 4
                if tok_w2use.get(sl) is not None:
                    p.wait("gpsimd", "pe", tok_w2use[sl])
                elif DO_B and sl >= 2:
                    p.wait("gpsimd", "pe", tok_vpmm[(3, 7)])
                nw2[sl] += 1
                p.do("gpsimd", lambda gg, e=e, g=g, hh=hh, sl=sl: gg.dma_start(
                    w2sl[:, sl * 1024:(sl + 1) * 1024].rearrange(
                        "p (c n) -> p c n", c=2),
                    d_w2[e, 2 * g:2 * g + 2, :, hh * 512:(hh + 1) * 512].rearrange(
                        "c p n -> p c n")),
                    inc=DW2[sl], inc_by=16)
                for f1c in (2 * g, 2 * g + 1):
                    p.wait("tensor", DW2[sl], 16 * nw2[sl])
                    p.wait("tensor", "act", tok_gelu[(half, e, f1c)])
                    for tl in range(8):
                        if f1c == 0:
                            if tl < 4:
                                wait_pair(tl // 2)
                            if last_sweep[0] is not None and \
                                    (last_sweep[0], tl) in tok_mix:
                                p.wait("tensor", "dve",
                                       tok_mix[(last_sweep[0], tl)])
                            if hh == 0 and tl >= 4 and half == 0 and e == 0:
                                p.wait("tensor", "dve", tok_wmap_all)
                                p.wait("tensor", "dve", tok_xnF_all)
                        want_inc = (f1c == NF1C - 1) or (f1c % 2 == 1 and tl == 7)
                        tok = p.do("tensor",
                                   lambda t, f1c=f1c, tl=tl, sl=sl, hh=hh:
                                   t.matmul(
                                       W2B[tl],
                                       vg[:, f1c * 1024 + tl * 128:
                                          f1c * 1024 + (tl + 1) * 128],
                                       w2sl[:, sl * 1024 + (f1c % 2) * 512:
                                            sl * 1024 + (f1c % 2) * 512 + 512],
                                       start=(f1c == 0), stop=(f1c == NF1C - 1)),
                                   inc=("pe" if want_inc else None))
                        if f1c == NF1C - 1:
                            tt = half * 8 + tl
                            p.wait("vector", "pe", tok)
                            tok_mix[(sweep, tl)] = p.do(
                                "vector",
                                lambda v, tl=tl, hh=hh, tt=tt, e=e:
                                v.scalar_tensor_tensor(
                                    mixed[:, tl * 1024 + hh * 512:
                                          tl * 1024 + hh * 512 + 512],
                                    W2B[tl],
                                    wmap_sb[:, tt * E + e: tt * E + e + 1],
                                    mixed[:, tl * 1024 + hh * 512:
                                          tl * 1024 + hh * 512 + 512],
                                    ALU.mult, ALU.add), inc="dve")
                            if tl < 4:
                                set_pair(tl // 2, "dve", tok_mix[(sweep, tl)])
                        tok_w2use[sl] = tok
                    tok_w2_reader[f1c] = tok
            last_sweep[0] = sweep

    def emit_final(half):
        for tl in range(8):
            tt = half * 8 + tl
            par = tt % 2
            if nst[par] > 0:
                p.wait("sync", DO[par], 16 * nst[par])
            p.wait("sync", "dve", tok_xnt[NTT - 2 + par])
            if tt >= 2 and (tt - 2) in tok_fadd:
                p.wait("sync", "dve", tok_fadd[tt - 2])
            nx[par] += 1
            p.do("sync", lambda sy, tt=tt, par=par: sy.dma_start(
                x_rot[:, par * 1024:(par + 1) * 1024],
                d_x[tt * 128:(tt + 1) * 128, :]), inc=DX[par], inc_by=16)
            p.wait("vector", DX[par], 16 * nx[par])
            for hh in range(2):
                p.wait("vector", "dve", tok_mix[((half, E - 1, hh), tl)])
            tok_fadd[tt] = p.do("vector", lambda v, par=par, tl=tl: v.tensor_add(
                x_rot[:, par * 1024:(par + 1) * 1024],
                mixed[:, tl * 1024:(tl + 1) * 1024],
                x_rot[:, par * 1024:(par + 1) * 1024]), inc="dve")
            p.wait("sync", "dve", tok_fadd[tt])
            nst[par] += 1
            p.do("sync", lambda sy, tt=tt, par=par: sy.dma_start(
                d_out[tt * 128:(tt + 1) * 128, :],
                x_rot[:, par * 1024:(par + 1) * 1024]), inc=DO[par], inc_by=16)

    vpq = []
    h1q = []
    last_kind = ["h1"]

    def pump():
        a = vpq if (last_kind[0] == "h1" and vpq) or not h1q else h1q
        if a:
            last_kind[0] = "vp" if a is vpq else "h1"
            a.pop(0)()

    DXP = ("dxp", "dxq")
    n_dxp = {0: 0, 1: 0}
    dxp_need = {}
    tok_brstd = {}

    def emit_pair_prestage(q):
        qp = q % 2
        for t2 in range(2):
            tt = 2 * q + t2
            if q >= 2:
                p.wait("sync", "dve", tok_xnt[2 * (q - 2) + t2])
            n_dxp[qp] += 1
            s = 2 * qp + t2
            p.do("sync", lambda sy, tt=tt, s=s, qp=qp: sy.dma_start(
                xst[:, s * 1024:(s + 1) * 1024],
                d_x[tt * 128:(tt + 1) * 128, :]), inc=DXP[qp], inc_by=16)
        dxp_need[q] = 16 * n_dxp[qp]

    def emit_pair_stats(q):
        qp = q % 2
        p.wait("vector", DXP[qp], dxp_need[q])
        if q >= 2:
            p.wait("vector", "act", tok_brstd[q - 2])
        tok = None
        for t2 in range(2):
            s = 2 * qp + t2
            xt = xst[:, s * 1024:(s + 1) * 1024]
            c = 2 * qp + t2
            p.do("vector", lambda v, xt=xt: v.reduce_sum(st[:, 0:1], xt, axis=AX))
            p.do("vector", lambda v, c=c: v.tensor_scalar_mul(
                tsc[:, c:c + 1], st[:, 0:1], 1.0 / D))
            p.do("vector", lambda v, xt=xt: v.scalar_tensor_tensor(
                xsq, xt, 1.0, xt, ALU.bypass, ALU.mult, accum_out=st[:, 2:3]))
            p.do("vector", lambda v: v.tensor_scalar_mul(st[:, 3:4], st[:, 2:3],
                                                         1.0 / D))
            tok = p.do("vector", lambda v, c=c: v.scalar_tensor_tensor(
                tsc[:, 4 + c:5 + c], tsc[:, c:c + 1], tsc[:, c:c + 1],
                st[:, 3:4], ALU.mult, ALU.subtract),
                inc=(None if t2 < 1 else "dve"))
        p.wait("scalar", "dve", tok)
        b = 4 + 2 * qp
        p.do("scalar", lambda s_, b=b: s_.activation(
            tsc[:, 8 + b:10 + b], tsc[:, b:b + 2], AF.Ln,
            scale=-1.0, bias=eps_col))
        tok_brstd[q] = p.do("scalar", lambda s_, b=b: s_.activation(
            tsc[:, 4 + b:6 + b], tsc[:, 8 + b:10 + b], AF.Exp,
            scale=-0.5), inc="act")

    emit_pair_prestage(0)
    emit_pair_prestage(1)
    emit_pair_stats(0)
    emit_pair_stats(1)

    for tt in range(NTT):
        q, t2 = tt // 2, tt % 2
        qp = q % 2
        s = 2 * qp + t2
        emit_A(tt, pump, stats=(xst[:, s * 1024:(s + 1) * 1024],
                                tsc[:, s:s + 1], tsc[:, 8 + s:9 + s],
                                tok_brstd[q]))
        if t2 == 1 and q + 2 < 8:
            emit_pair_prestage(q + 2)
            emit_pair_stats(q + 2)
        if DO_B and tt % 4 == 3:
            ts = tt // 4
            vpq.extend([(lambda ts=ts, vc=vc: emit_vp_unit(ts, vc))
                        for vc in range(NFC)])
        if DO_C and tt == 7:
            h1q.extend([(lambda g=g: emit_h1_group(0, 0, g))
                        for g in range(NF1C // 2)])
    p.wait("scalar", "dve", tok_p1box[0])
    tok_te = p.do("scalar", lambda s: s.activation(ecols, dcols, AF.Exp),
                  inc="act")
    p.wait("vector", "act", tok_te)
    for tt in range(NTT):
        emit_top2_p3(tt)
    while vpq or h1q:
        pump()
    for tt in range(NTT):
        emit_top2_wt(tt)
    tok_xnF_all = tok_xnFev[NTT - 1]
    tok_wmap_all = tok_wmapTev[NTT - 1]
    if DO_B:
        tok_optF_all = tok_vev[8 * 4 - 1]
    for half in (range(2) if DO_C else []):
        if half == 1:
            emit_h1(1, 0)
        emit_outproj(half)
        emit_w2(half, 0)
        for e in range(1, E):
            emit_h1(half, e)
            emit_w2(half, e)
        emit_final(half)

    if dbg:
        p.wait("gpsimd", "dve", p.cnt["dve"])
        p.wait("gpsimd", "act", p.cnt["act"])
        p.wait("gpsimd", "pe", p.cnt["pe"])
        dumps = [(d_dbg_wmap, wmap_sb), (d_dbg_xnF, xnF)]
        if DO_B:
            dumps.append((d_dbg_optF, optF))
        for dst, s_ in dumps:
            p.do("gpsimd", lambda g, dst=dst, s_=s_: g.dma_start(dst, s_),
                 inc="ddbg", inc_by=16)
        p.wait("gpsimd", "ddbg", p.cnt["ddbg"])

    p.wait("sync", DO[0], 16 * nst[0])
    p.wait("sync", DO[1], 16 * nst[1])

    with nc.Block() as blk:
        @blk.tensor
        def _(En):
            for f in p.streams["tensor"]:
                f(En)

        @blk.scalar
        def _(En):
            for f in p.streams["scalar"]:
                f(En)

        @blk.vector
        def _(En):
            for f in p.streams["vector"]:
                f(En)

        @blk.gpsimd
        def _(En):
            for f in p.streams["gpsimd"]:
                f(En)

        @blk.sync
        def _(En):
            for f in p.streams["sync"]:
                f(En)

    return nc


_CACHE = {}


def _prep(inputs):
    w_in = np.asarray(inputs["L0_attn_w_in"], np.float32)
    b_in = np.asarray(inputs["L0_attn_b_in"], np.float32)
    w_out = np.asarray(inputs["L0_attn_w_out"], np.float32)
    b_out = np.asarray(inputs["L0_attn_b_out"], np.float32)
    rw = np.asarray(inputs["L0_ffn_router_w"], np.float32)
    w1 = np.asarray(inputs["L0_ffn_w1"], np.float32)
    b1 = np.asarray(inputs["L0_ffn_b1"], np.float32)
    w2 = np.asarray(inputs["L0_ffn_w2"], np.float32)
    b2 = np.asarray(inputs["L0_ffn_b2"], np.float32)

    Ar = w_in.reshape(NH, 192, D)
    wvf2 = Ar[:, 128:192, :].reshape(D, D).T
    wvf = np.ascontiguousarray(
        wvf2.reshape(NFC, 128, NFC, 128).transpose(2, 0, 1, 3))
    bvc = np.ascontiguousarray(
        b_in.reshape(NH, 192)[:, 128:192].reshape(NFC, 128).T)
    wo = np.ascontiguousarray(w_out.T)
    b2b = np.ascontiguousarray(np.concatenate([b2, b_out[None, :]], axis=0))
    rout = np.ascontiguousarray(rw.T)
    w1r = np.ascontiguousarray(
        w1.reshape(E, NF1C, 128, NFC, 128).transpose(0, 1, 4, 3, 2))
    b1c = np.ascontiguousarray(b1.reshape(E, NF1C, 128).transpose(2, 1, 0))
    w2r = np.ascontiguousarray(w2.reshape(E, D, NF1C, 128).transpose(0, 2, 3, 1))
    iden = np.eye(128, dtype=np.float32)
    return dict(wvf=wvf, bvc=bvc, wo=wo, b2b=b2b, rout=rout,
                w1r=w1r, b1c=b1c, w2r=w2r, iden=iden)


def kernel(**inputs):
    x = np.ascontiguousarray(np.asarray(inputs["L0_attn_x"], np.float32))
    w = _prep(inputs)
    if "nc" not in _CACHE:
        _CACHE["nc"] = build_nc()
    nc = _CACHE["nc"]
    in_maps = [dict(x=np.ascontiguousarray(x[c]), **w) for c in range(8)]
    res = run_bass_kernel_spmd(nc, in_maps, core_ids=list(range(8)))
    return np.stack([np.asarray(res.results[c]["out"]) for c in range(8)], axis=0)
